# version 02 (final)
# baseline (speedup 1.0000x reference)
import contextlib
import math
import os
import numpy as np
import ml_dtypes

import concourse.bass as bass
import concourse.mybir as mybir
from concourse.bass_utils import run_bass_kernel_spmd
from concourse.alu_op_type import AluOpType

F32 = mybir.dt.float32
BF16 = mybir.dt.bfloat16
AF = mybir.ActivationFunctionType
AX = mybir.AxisListType
OP = AluOpType

B, S, D = 4, 4096, 4096
NH, HD = 32, 128
DQKV = 4352
H_FFN = 11008
TOK = 2048
NSLOT, QT = 4, 512
FC = D // 128
HC = H_FFN // 128
STAB = -20.0
SCALE = float(HD ** -0.5)
EPS = 1e-5

W12_CHUNKS = [11, 11, 11, 11, 11, 11, 10, 10]
W12_PAD = 11
W12_ROWS = W12_PAD * 128

SLOT_Q0 = [(3072, 3584), (2048, 2560), (1024, 1536), (0, 512)]
SLOT_L = [q1 // 128 + 4 for (_, q1) in SLOT_Q0]
SLOT_V = [q0 // 128 for (q0, _) in SLOT_Q0]
NVAR = 8

_slopes64 = (2.0 ** (-2.0 ** -(math.log2(32) - 3))) ** np.arange(1, 33, dtype=np.float64)
SLOPES = _slopes64.astype(np.float32)

NF4_CODE = np.array([
    -1.0, -0.6961928009986877, -0.5250730514526367, -0.39491748809814453,
    -0.28444138169288635, -0.18477343022823334, -0.09105003625154495, 0.0,
    0.07958029955625534, 0.16093020141124725, 0.24611230194568634,
    0.33791524171829224, 0.44070982933044434, 0.5626170039176941,
    0.7229568362236023, 1.0], dtype=np.float32)
NF4_BOUNDS = ((NF4_CODE[:-1] + NF4_CODE[1:]) * np.float32(0.5)).astype(np.float32)
NF4_DELTA = np.diff(NF4_CODE.astype(np.float64))


def skip_radius(h):
    return int(math.ceil(135.0 / (128.0 * float(SLOPES[h])))) + 1


def head_sched(h, s):
    return max(0, SLOT_V[s] - skip_radius(h)), SLOT_L[s]


ENGS = ["sync", "vector", "scalar", "tensor", "gpsimd"]
Z = ("vector", 0)


class Sem:
    def __init__(self, handle):
        self.h = handle
        self.n = 0


class Sched:

    def __init__(self, nc, enter):
        self.nc = nc
        self.enter = enter
        self.ops = {n: [] for n in ENGS}
        self.sem = {}
        for n in ENGS:
            self.sem[n] = Sem(enter(nc.semaphore("sem_" + n)))
        self._waited = {n: {} for n in ENGS}

    def _getsem(self, key):
        if key not in self.sem:
            self.sem[key] = Sem(self.enter(self.nc.semaphore("sem_" + key)))
        return self.sem[key]

    def do(self, qn, fn, waits=(), inc=1, inc_sem=None):
        w = []
        seen = self._waited[qn]
        for ev in waits:
            if ev is None:
                continue
            sname, val = ev
            if val <= 0 or seen.get(sname, -1) >= val:
                continue
            seen[sname] = val
            w.append((self._getsem(sname), val))
        key = qn if inc_sem is None else inc_sem
        sem = self._getsem(key) if inc else None
        self.ops[qn].append((w, fn, sem, inc))
        if inc:
            sem.n += inc
        return (key, sem.n) if inc else (qn, self.sem[qn].n)

    def dma(self, qn, out, in_, waits=(), stream=None, parity=None):
        key = qn if stream is None else (stream if parity is None else f"{stream}{parity % 2}")
        return self.do(qn, lambda e, o=out, i=in_: e.dma_start(out=o, in_=i), waits,
                       inc=16, inc_sem=key)

    def total(self, key):
        return (key, self._getsem(key).n)

    def chain(self, qn, fns, waits=()):
        ev = None
        for k, fn in enumerate(fns):
            ev = self.do(qn, fn, waits=(list(waits) if k == 0 else [ev]))
        return ev

    def marks(self):
        return [(n, self.sem[n].n) for n in ENGS]

    def finalize(self, block):
        for n in ENGS:
            ops = self.ops[n]
            if not ops:
                continue

            def runner(eng, ops=ops):
                for waits, fn, sem, inc in ops:
                    for s, val in waits:
                        eng.wait_ge(s.h, val)
                    instr = fn(eng)
                    if sem is not None:
                        instr.then_inc(sem.h, inc)
            getattr(block, n)(runner)


def MM(out, lhsT, rhs, start, stop):
    return lambda e: e.matmul(out, lhsT, rhs, start=start, stop=stop, skip_group_check=True)


def build_prep():
    nc = bass.Bass("TRN2", target_bir_lowering=False)
    wqkv_s = nc.dram_tensor("wqkv_s", [512, DQKV], F32, kind="ExternalInput").ap()
    wo_s = nc.dram_tensor("wo_s", [512, D], F32, kind="ExternalInput").ap()
    w1_s = nc.dram_tensor("w1_s", [W12_ROWS, D], F32, kind="ExternalInput").ap()
    w2_s = nc.dram_tensor("w2_s", [W12_ROWS, D], F32, kind="ExternalInput").ap()
    w3_s = nc.dram_tensor("w3_s", [512, H_FFN], F32, kind="ExternalInput").ap()

    wqkv_b = nc.dram_tensor("wqkv_b", [512, DQKV], BF16, kind="ExternalOutput").ap()
    wo_b = nc.dram_tensor("wo_b", [512, D], BF16, kind="ExternalOutput").ap()
    dq1 = nc.dram_tensor("dq1", [W12_ROWS, D], BF16, kind="ExternalOutput").ap()
    dq2 = nc.dram_tensor("dq2", [W12_ROWS, D], BF16, kind="ExternalOutput").ap()
    dq3 = nc.dram_tensor("dq3", [512, H_FFN], BF16, kind="ExternalOutput").ap()
    rs1 = nc.dram_tensor("rs1", [W12_ROWS, 1], F32, kind="ExternalOutput").ap()
    rs2 = nc.dram_tensor("rs2", [W12_ROWS, 1], F32, kind="ExternalOutput").ap()
    csp = nc.dram_tensor("csp", [1, DQKV], F32, kind="ExternalOutput").ap()

    with contextlib.ExitStack() as st:
        e = st.enter_context
        q = Sched(nc, e)
        block = e(nc.Block())

        def emit_prep_casts():
            with contextlib.ExitStack() as ph:
                e2 = ph.enter_context
                wt = [e2(nc.sbuf_tensor(f"awt{i}", [128, DQKV], F32)) for i in range(2)]
                cast4 = e2(nc.sbuf_tensor("acast4", [128, 4 * DQKV], BF16))
                onesb = e2(nc.sbuf_tensor("aonesb", [128, 1], BF16))
                csrow = e2(nc.sbuf_tensor("acsrow", [1, DQKV], F32))
                ps_cs = e2(nc.psum_tensor("aps_cs", [1, 512], F32))

                ones_ev = q.do("vector", lambda e_: e_.memset(onesb[:], 1.0))
                cast_ev = [None] * 4
                st_q = [None] * 4
                for i in range(4):
                    ld = q.dma("sync", wt[i % 2][:], wqkv_s[i * 128:(i + 1) * 128, :],
                               waits=[cast_ev[i - 2] if i >= 2 else None],
                               stream="pw", parity=i)
                    cast_ev[i] = q.do("vector",
                                      lambda e_, i=i: e_.tensor_copy(
                                          cast4[:, i * DQKV:(i + 1) * DQKV], wt[i % 2][:]),
                                      waits=[ld])
                    st_q[i] = q.dma("gpsimd", wqkv_b[i * 128:(i + 1) * 128, :],
                                    cast4[:, i * DQKV:(i + 1) * DQKV], waits=[cast_ev[i]],
                                    stream="pst")
                evac_ev = None
                for sp in range(9):
                    c0, w = sp * 512, min(512, DQKV - sp * 512)
                    mmv = None
                    for i in range(4):
                        mmv = q.do("tensor",
                                   MM(ps_cs[:, :w], onesb[:],
                                      cast4[:, i * DQKV + c0:i * DQKV + c0 + w],
                                      i == 0, i == 3),
                                   waits=[cast_ev[i], ones_ev, evac_ev],
                                   inc=(1 if i == 3 else 0))
                    evac_ev = q.do("vector",
                                   lambda e_, c0=c0, w=w: e_.tensor_copy(csrow[:, c0:c0 + w],
                                                                         ps_cs[:, :w]),
                                   waits=[mmv])
                q.dma("gpsimd", csp[:, :], csrow[:], waits=[evac_ev], stream="pst")

                cev = evac_ev
                wo_st = {}
                for i in range(4):
                    dst = cast4[:, (i % 2) * DQKV:(i % 2) * DQKV + D].bitcast(BF16)
                    ld = q.dma("sync", wt[i % 2][:, :D], wo_s[i * 128:(i + 1) * 128, :],
                               waits=[cast_ev[3], cev], stream="pw", parity=i)
                    cev = q.do("vector",
                               lambda e_, i=i, dst=dst: e_.tensor_copy(dst, wt[i % 2][:, :D]),
                               waits=[ld, evac_ev, q.total("pst")])
                    wo_st[i] = q.dma("gpsimd", wo_b[i * 128:(i + 1) * 128, :], dst,
                                     waits=[cev], stream="pst")
                last_wo_st = wo_st[3]

            return cev, last_wo_st

        cev, last_wo_st = emit_prep_casts()

        def emit_prep_dequant():
            WMAX = H_FFN // 2
            with contextlib.ExitStack() as ph:
                e2 = ph.enter_context
                wt = [e2(nc.sbuf_tensor(f"dwt{i}", [128, WMAX], F32)) for i in range(2)]
                nm = e2(nc.sbuf_tensor("dnm", [128, WMAX], F32))
                am = e2(nc.sbuf_tensor("dam", [128, WMAX // 64], F32))
                rc = e2(nc.sbuf_tensor("drc", [128, WMAX // 64], F32))
                acc = e2(nc.sbuf_tensor("dacc", [128, WMAX], F32))
                tt = e2(nc.sbuf_tensor("dtt", [128, WMAX], F32))
                dqf = e2(nc.sbuf_tensor("ddqf", [128, WMAX], F32))
                dqo = [e2(nc.sbuf_tensor(f"ddqo{i}", [128, WMAX], BF16)) for i in range(2)]
                rsum = e2(nc.sbuf_tensor("drsum", [128, 2], F32))

                jobs = []
                for i in range(W12_PAD):
                    jobs.append((w1_s[i * 128:(i + 1) * 128, :], D,
                                 dq1[i * 128:(i + 1) * 128, :], rs1[i * 128:(i + 1) * 128, :]))
                for i in range(W12_PAD):
                    jobs.append((w2_s[i * 128:(i + 1) * 128, :], D,
                                 dq2[i * 128:(i + 1) * 128, :], rs2[i * 128:(i + 1) * 128, :]))
                for i in range(4):
                    for hh in range(2):
                        jobs.append((w3_s[i * 128:(i + 1) * 128, hh * WMAX:(hh + 1) * WMAX],
                                     WMAX, dq3[i * 128:(i + 1) * 128, hh * WMAX:(hh + 1) * WMAX],
                                     None))

                wk_ev = [None] * len(jobs)
                st_ev = [None] * len(jobs)
                for idx, (src, cols, d_dq, d_rs) in enumerate(jobs):
                    ld = q.dma("sync", wt[idx % 2][:, :cols], src,
                               waits=[wk_ev[idx - 2] if idx >= 2 else cev,
                                      q.total("pst") if idx < 2 else None],
                               stream="pd", parity=idx)

                    def mk_dq_fns(idx=idx, cols=cols, has_rs=(d_rs is not None)):
                        nblk = cols // 64
                        w = wt[idx % 2][:, :cols]
                        o = dqo[idx % 2]
                        w3d = w.rearrange("p (b k) -> p b k", k=64)
                        rcb = rc[:, :nblk].rearrange("p (b o) -> p b o", o=1).broadcast_to([128, nblk, 64])
                        amb = am[:, :nblk].rearrange("p (b o) -> p b o", o=1).broadcast_to([128, nblk, 64])
                        fns = [
                            lambda e_: e_.tensor_reduce(am[:, :nblk], w3d, AX.X, OP.max,
                                                        apply_absolute_value=True),
                            lambda e_: e_.reciprocal(rc[:, :nblk], am[:, :nblk]),
                            lambda e_: e_.tensor_tensor(nm[:, :cols].rearrange("p (b k) -> p b k", k=64),
                                                        w3d, rcb, OP.mult),
                            lambda e_: e_.tensor_scalar(acc[:, :cols], nm[:, :cols],
                                                        float(NF4_BOUNDS[0]), float(NF4_DELTA[0]),
                                                        OP.is_gt, OP.mult),
                        ]
                        for i in range(1, 15):
                            fns.append(lambda e_, i=i: e_.tensor_scalar(
                                tt[:, :cols], nm[:, :cols], float(NF4_BOUNDS[i]),
                                float(NF4_DELTA[i]), OP.is_gt, OP.mult))
                            fns.append(lambda e_: e_.tensor_tensor(acc[:, :cols], acc[:, :cols],
                                                                   tt[:, :cols], OP.add))
                        fns.append(lambda e_: e_.scalar_tensor_tensor(
                            dqf[:, :cols].rearrange("p (b k) -> p b k", k=64),
                            acc[:, :cols].rearrange("p (b k) -> p b k", k=64),
                            -1.0, amb, OP.add, OP.mult))
                        if has_rs:
                            fns.append(lambda e_: e_.tensor_reduce(
                                rsum[:, idx % 2:idx % 2 + 1], dqf[:, :cols], AX.X, OP.add))
                        fns.append(lambda e_: e_.tensor_copy(o[:, :cols], dqf[:, :cols]))
                        return fns

                    wk_ev[idx] = q.chain("vector", mk_dq_fns(),
                                         waits=[ld, q.total("pst") if idx >= 2 else None])
                    sev = q.dma("gpsimd", d_dq, dqo[idx % 2][:, :cols], waits=[wk_ev[idx]],
                                stream="pst")
                    if d_rs is not None:
                        sev = q.dma("gpsimd", d_rs, rsum[:, idx % 2:idx % 2 + 1], stream="pst")
                    st_ev[idx] = sev


        emit_prep_dequant()

        q.do("gpsimd", lambda e_: e_.engine_nop(), waits=[q.total("pst")])
        q.finalize(block)
    return nc


def build_main(debug=(), upto='p4'):
    nc = bass.Bass("TRN2", target_bir_lowering=False)
    xfull = nc.dram_tensor("xfull", [S, D], F32, kind="ExternalInput").ap()
    xq = nc.dram_tensor("xq", [TOK, D], F32, kind="ExternalInput").ap()
    td_in = nc.dram_tensor("td_in", [128, QT], F32, kind="ExternalInput").ap()
    mvar_in = nc.dram_tensor("mvar_in", [NSLOT, 128, NVAR * QT], F32, kind="ExternalInput").ap()
    kqb_in = nc.dram_tensor("kqb_in", [128, NSLOT * 32], F32, kind="ExternalInput").ap()
    idf = nc.dram_tensor("idf", [128, 128], F32, kind="ExternalInput").ap()
    idb = nc.dram_tensor("idb", [128, 128], BF16, kind="ExternalInput").ap()

    SLIM = bool(os.environ.get("SLIMINPUTS"))
    wqg = nc.dram_tensor("wqg", [34, 128, FC * 128], BF16, kind="ExternalInput").ap()
    wog = nc.dram_tensor("wog", [FC, 128, FC * 128], BF16, kind="ExternalInput").ap() if not SLIM else None
    w1t = nc.dram_tensor("w1t", [HC, 128, FC * 128], BF16, kind="ExternalInput").ap() if not SLIM else None
    w2t = nc.dram_tensor("w2t", [HC, 128, FC * 128], BF16, kind="ExternalInput").ap() if not SLIM else None
    w3t = nc.dram_tensor("w3t", [FC, 128, HC * 128], BF16, kind="ExternalInput").ap() if not SLIM else None
    cspart = nc.dram_tensor("cspart", [8, DQKV], F32, kind="ExternalInput").ap()
    cs1f = nc.dram_tensor("cs1f", [1, HC * 128], F32, kind="ExternalInput").ap()
    cs2f = nc.dram_tensor("cs2f", [1, HC * 128], F32, kind="ExternalInput").ap()

    out_q = nc.dram_tensor("out_q", [TOK, D], F32, kind="ExternalOutput").ap()

    xprime = nc.dram_tensor("xprime", [FC, 128, TOK], F32).ap()
    xprime_b = nc.dram_tensor("xprime_b", [FC, 128, TOK], BF16).ap()
    xt_scr = nc.dram_tensor("xt_scr", [FC, 128, QT], F32).ap()
    h_scr = nc.dram_tensor("h_scr", [HC, 128, TOK], BF16).ap()

    dbg = {}
    for name, shape, dt in [("xprime", [FC, 128, TOK], F32), ("kt", [128, S], BF16),
                            ("v", [128, S], BF16), ("qt", [128, 2 * QT], BF16),
                            ("attnT", [128, NH * QT], BF16), ("h", [HC, 128, TOK], BF16),
                            ("xtb", [128, FC * QT], BF16)]:
        if name in debug:
            dbg[name] = nc.dram_tensor("dbg_" + name, shape, dt, kind="ExternalOutput").ap()

    with contextlib.ExitStack() as st:
        e = st.enter_context
        q = Sched(nc, e)
        block = e(nc.Block())

        identf = e(nc.sbuf_tensor("identf", [128, 128], F32))
        identb = e(nc.sbuf_tensor("identb", [128, 128], BF16))
        onesb = e(nc.sbuf_tensor("onesb", [128, 1], BF16))
        onesf = e(nc.sbuf_tensor("onesf", [1, 128], F32))
        epsc = e(nc.sbuf_tensor("epsc", [128, 1], F32))
        var2 = e(nc.sbuf_tensor("var2", [1, TOK], F32))
        nm2row = e(nc.sbuf_tensor("nm2row", [1, TOK], BF16))

        p12 = contextlib.ExitStack()
        e12 = p12.enter_context
        ktsb = e12(nc.sbuf_tensor("ktsb", [128, S], BF16))
        vsb = e12(nc.sbuf_tensor("vsb", [128, S], BF16))
        tdsb = e12(nc.sbuf_tensor("tdsb", [128, QT], F32))
        kqbsb = e12(nc.sbuf_tensor("kqbsb", [128, NSLOT * 32], F32))
        csq_sb = e12(nc.sbuf_tensor("csq_sb", [1, DQKV], BF16))

        def emit_setup():
            q.dma("gpsimd", identf[:], idf[:, :], stream="cst")
            q.dma("gpsimd", identb[:], idb[:, :], stream="cst")
            q.dma("gpsimd", tdsb[:], td_in[:, :], stream="cst")
            q.dma("gpsimd", kqbsb[:], kqb_in[:, :], stream="cst")
            q.do("vector", lambda e_: e_.memset(onesb[:], 1.0))
            q.do("vector", lambda e_: e_.memset(epsc[:], EPS))
            ones_ev = q.do("vector", lambda e_: e_.memset(onesf[:], 1.0))
            cst_ev = q.total("cst")
            if os.environ.get("SKIPSETUPCS"):
                csq_ev = q.do("scalar", lambda e_: e_.copy(csq_sb[:, 0:128].bitcast(BF16), identb[:1, :].bitcast(BF16)),
                              waits=[cst_ev])
                setup_done = [csq_ev, ("vector", q.sem["vector"].n)]
                return cst_ev, ones_ev, csq_ev, setup_done
            with contextlib.ExitStack() as ph:
                e2 = ph.enter_context
                csp_sb = e2(nc.sbuf_tensor("csp_sb", [8, DQKV], F32))
                ones8f = e2(nc.sbuf_tensor("ones8f", [8, 1], F32))
                csf_sb = e2(nc.sbuf_tensor("csf_sb", [1, DQKV], F32))
                ps_c = e2(nc.psum_tensor("ps_c", [1, 512], F32))
                lm = q.dma("gpsimd", csp_sb[:], cspart[:, :], stream="cst")
                cst_ev = q.total("cst")
                lm = cst_ev
                om = q.do("vector", lambda e_: e_.memset(ones8f[:], 1.0))
                evac = None
                for sp in range(9):
                    c0, w = sp * 512, min(512, DQKV - sp * 512)
                    mk = q.do("tensor", MM(ps_c[:, :w], ones8f[:], csp_sb[:, c0:c0 + w],
                                           True, True),
                              waits=[lm, om, evac])
                    evac = q.do("vector",
                                lambda e_, c0=c0, w=w: e_.tensor_copy(csf_sb[:, c0:c0 + w],
                                                                      ps_c[:, :w]),
                                waits=[mk])
                csq_ev = q.do("vector", lambda e_: e_.tensor_copy(csq_sb[:], csf_sb[:]),
                          waits=[evac])
            setup_done = [csq_ev, ("tensor", q.sem["tensor"].n),
                          ("vector", q.sem["vector"].n)]
            return cst_ev, ones_ev, csq_ev, setup_done

        cst_ev, ones_ev, csq_ev, setup_done = emit_setup()

        P1LEVEL = int(os.environ.get("P1LEVEL", "9"))
        P1SPANS = int(os.environ.get("P1SPANS", "8"))

        def emit_p1():
            with contextlib.ExitStack() as ph:
                e2 = ph.enter_context
                xtok = [e2(nc.sbuf_tensor(f"p1xt{i}", [128, D], F32)) for i in range(4)]
                xtb = e2(nc.sbuf_tensor("p1xtb", [128, FC * QT], BF16))
                sqb = e2(nc.sbuf_tensor("p1sqb", [128, 2 * QT], BF16))
                murow = e2(nc.sbuf_tensor("p1mu", [1, QT], F32))
                nmrow_b = e2(nc.sbuf_tensor("p1nmb", [1, QT], BF16))
                varrow = e2(nc.sbuf_tensor("p1var", [1, QT], F32))
                musq = e2(nc.sbuf_tensor("p1musq", [1, QT], F32))
                rrow = e2(nc.sbuf_tensor("p1r", [1, QT], F32))
                rbc = e2(nc.sbuf_tensor("p1rbc", [128, QT], F32))
                vtmp = e2(nc.sbuf_tensor("p1vtmp", [128, QT], BF16))
                wkv_sb = e2(nc.sbuf_tensor("p1wkv", [128, 2 * FC * 128], BF16))
                ps_tr = e2(nc.psum_tensor("p1ps_tr", [128, 1024], F32))
                ps_mu = e2(nc.psum_tensor("p1ps_mu", [1, QT], F32))
                ps_ss = e2(nc.psum_tensor("p1ps_ss", [1, QT], F32))
                ps_kv = e2(nc.psum_tensor("p1ps_kv", [128, 2 * QT], F32))
                ps_bc = e2(nc.psum_tensor("p1ps_bc", [128, QT], F32))
                ps_vt = e2(nc.psum_tensor("p1ps_vt", [128, 512], BF16))

                wkv_ev = [None, None]
                if not os.environ.get("SKIPWKV"):
                    for kvi in range(2):
                        wkv_ev[kvi] = q.dma(
                            "sync",
                            wkv_sb[:, kvi * FC * 128:(kvi + 1) * FC * 128],
                            wqg[32 + kvi, :, :], stream="wkv",
                            waits=setup_done if kvi == 0 else ())
                wkv_tot = q.total("wkv")

                NSPAN = P1SPANS
                ev_trevac = {}
                ev_kv_evac = [None, None]
                ev_srow = ev_srow2 = None
                ev_rbc = ev_rbc_free = None
                ev_vtmp_free = ev_vt_last = None
                ev_kt_last = None
                ev_nmb_free = None
                ev_rbc_dve_free = None
                ev_ss_g = {}
                for v in range(NSPAN):
                    prev_trevac = dict(ev_trevac)
                    for j in range(4):
                        q.dma("sync", xtok[j][:],
                              xfull[v * QT + j * 128: v * QT + (j + 1) * 128, :],
                              waits=[prev_trevac.get((FC - 1, 3))], stream="x1")
                    xm_tot = q.total("x1")
                    ev_trevac = {}
                    for f in range(FC):
                        b0 = (f % 2) * 512
                        for j in range(4):
                            tr = q.do("tensor",
                                      lambda e_, f=f, j=j, b0=b0: e_.transpose(
                                          ps_tr[:, b0 + j * 128:b0 + (j + 1) * 128],
                                          xtok[j][:, f * 128:(f + 1) * 128], identf[:]),
                                      waits=[xm_tot, cst_ev,
                                             ev_trevac.get((f - 2, 3)) if j == 0 else None],
                                      inc=(1 if j == 3 else 0))
                        ev_trevac[(f, 3)] = q.do(
                            "scalar",
                            lambda e_, f=f, b0=b0: e_.copy(
                                xtb[:, f * QT:(f + 1) * QT], ps_tr[:, b0:b0 + 512]),
                            waits=[tr])
                    if P1LEVEL < 2:
                        ev_xtb_dummy = ev_trevac[(FC - 1, 3)]
                        ev_kt_last = ev_vt_last = ev_xtb_dummy
                        continue
                    ev_sq = {}
                    ev_mu = ev_ss = None
                    for f in range(FC):
                        ev_sq[f] = q.do("scalar",
                                        lambda e_, f=f: e_.activation(
                                            sqb[:, (f % 2) * QT:(f % 2 + 1) * QT],
                                            xtb[:, f * QT:(f + 1) * QT], AF.Square),
                                        waits=[ev_trevac[(f, 3)], ev_ss_g.get(f - 2)])
                        ev_mu = q.do("tensor",
                                     MM(ps_mu[:], onesb[:], xtb[:, f * QT:(f + 1) * QT],
                                        f == 0, f == FC - 1),
                                     waits=[ev_trevac[(f, 3)], ones_ev,
                                            ev_srow if f == 0 else None],
                                     inc=(1 if f == FC - 1 else 0))
                        ev_ss = q.do("tensor",
                                     MM(ps_ss[:], onesb[:], sqb[:, (f % 2) * QT:(f % 2 + 1) * QT],
                                        f == 0, f == FC - 1),
                                     waits=[ev_sq[f]])
                        ev_ss_g[f] = ev_ss
                    ev_ss_g = {FC - 2: ev_ss_g[FC - 2], FC - 1: ev_ss}

                    if P1LEVEL < 3:
                        ev_kt_last = ev_vt_last = ev_ss
                        continue
                    ev_srow = q.chain("vector", [
                        lambda e_: e_.tensor_scalar(murow[:], ps_mu[:], -1.0 / D, None, OP.mult),
                        lambda e_: e_.tensor_scalar(varrow[:], ps_ss[:], 1.0 / D, None, OP.mult),
                        lambda e_: e_.tensor_tensor(musq[:], murow[:], murow[:], OP.mult),
                        lambda e_: e_.tensor_tensor(varrow[:], varrow[:], musq[:], OP.subtract),
                        lambda e_: e_.tensor_copy(nmrow_b[:], murow[:]),
                    ], waits=[ev_mu, ev_ss, ev_srow2, ev_nmb_free])
                    ev_srow2 = ev_srow

                    if P1LEVEL < 4:
                        ev_kt_last = ev_vt_last = ev_srow2
                        continue
                    ev_bc = q.do("tensor", MM(ps_bc[:], onesf[:], varrow[:], True, True),
                                 waits=[ev_srow, ones_ev, ev_rbc_free])
                    ev_rbc = q.chain("scalar", [
                        lambda e_: e_.activation(rbc[:], ps_bc[:], AF.Ln, bias=epsc[:, 0:1]),
                        lambda e_: e_.activation(rbc[:], rbc[:], AF.Exp, scale=-0.5),
                    ], waits=[ev_bc, ev_rbc_dve_free])

                    if P1LEVEL < 5:
                        ev_kt_last = ev_vt_last = ev_rbc
                        continue
                    ev_kvc = [None, None]
                    for kvi in range(2):
                        for f in range(FC):
                            q.do("tensor",
                                 MM(ps_kv[:, kvi * QT:(kvi + 1) * QT],
                                    wkv_sb[:, (kvi * FC + f) * 128:(kvi * FC + f + 1) * 128],
                                    xtb[:, f * QT:(f + 1) * QT], f == 0, False),
                                 waits=[wkv_tot, ev_trevac[(f, 3)],
                                        ev_kv_evac[kvi] if f == 0 else None],
                                 inc=0)
                        ev_kvc[kvi] = q.do(
                            "tensor",
                            MM(ps_kv[:, kvi * QT:(kvi + 1) * QT],
                               csq_sb[:, 4096 + kvi * 128:4096 + (kvi + 1) * 128],
                               nmrow_b[:], False, True),
                            waits=[ev_srow2, csq_ev])
                    ev_kv_evac[0] = q.do("vector",
                                         lambda e_, v=v: e_.tensor_tensor(
                                             ktsb[:, v * QT:(v + 1) * QT], ps_kv[:, 0:QT],
                                             rbc[:], OP.mult),
                                         waits=[ev_kvc[0], ev_rbc])
                    ev_kv_evac[1] = q.do("vector",
                                         lambda e_: e_.tensor_tensor(
                                             vtmp[:], ps_kv[:, QT:2 * QT], rbc[:], OP.mult),
                                         waits=[ev_kvc[1], ev_vtmp_free])
                    ev_kt_last = ev_kv_evac[0]
                    ev_nmb_free = ev_kvc[1]
                    ev_rbc_free = ev_rbc
                    ev_rbc_dve_free = ev_kv_evac[1]
                    for j in range(4):
                        tr = q.do("tensor",
                                  lambda e_, j=j: e_.transpose(
                                      ps_vt[:, j * 128:(j + 1) * 128],
                                      vtmp[:, j * 128:(j + 1) * 128], identb[:]),
                                  waits=[ev_kv_evac[1], ev_vt_last if j == 0 else None],
                                  inc=(1 if j == 3 else 0))
                    ev_vt_last = q.do("vector",
                                      lambda e_, v=v: e_.tensor_copy(
                                          vsb[:, v * QT:(v + 1) * QT], ps_vt[:]),
                                      waits=[tr])
                    ev_vtmp_free = ev_vt_last
                if "kt" in dbg:
                    q.dma("gpsimd", dbg["kt"][:, :], ktsb[:], waits=[ev_kt_last], stream="dbgs")
                if "v" in dbg:
                    q.dma("gpsimd", dbg["v"][:, :], vsb[:], waits=[ev_vt_last], stream="dbgs")
            p1_pe = ("tensor", q.sem["tensor"].n)
            p1_dv = ("vector", q.sem["vector"].n)
            p1_sc = ("scalar", q.sem["scalar"].n)
            return ev_kt_last, ev_vt_last, p1_pe, p1_dv, p1_sc

        if upto == 'setup':
            q.do("gpsimd", lambda e_: e_.engine_nop(), waits=[csq_ev], inc=0)
            p12.close()
            q.finalize(block)
            return nc, dbg
        ev_kt_last, ev_vt_last, p1_pe, p1_dv, p1_sc = emit_p1()

        P2LEVEL = int(os.environ.get("P2LEVEL", "9"))
        P2A = int(os.environ.get("P2A", "9"))

        def emit_p2():
            with contextlib.ExitStack() as ph:
                e2 = ph.enter_context
                xtb = e2(nc.sbuf_tensor("p2xtb", [128, FC * QT], BF16))
                mvar = e2(nc.sbuf_tensor("p2mvar", [128, NVAR * QT], F32))
                sqb = e2(nc.sbuf_tensor("p2sqb", [128, 2 * QT], BF16))
                murow = e2(nc.sbuf_tensor("p2mu", [1, QT], F32))
                nmrow_b = e2(nc.sbuf_tensor("p2nmb", [1, QT], BF16))
                varrow = e2(nc.sbuf_tensor("p2var", [1, QT], F32))
                musq = e2(nc.sbuf_tensor("p2musq", [1, QT], F32))
                rrow = e2(nc.sbuf_tensor("p2r", [1, QT], F32))
                rbc = e2(nc.sbuf_tensor("p2rbc", [128, QT], F32))
                rsbc = e2(nc.sbuf_tensor("p2rsbc", [128, QT], F32))
                wstr = [e2(nc.sbuf_tensor(f"p2w{i}", [128, FC * 128], BF16)) for i in range(2)]
                qtb = [e2(nc.sbuf_tensor(f"p2qtb{i}", [128, QT], BF16)) for i in range(2)]
                bias_h = e2(nc.sbuf_tensor("p2bias", [128, 32], F32))

                EV = {"mvar_free": p1_pe, "xtok_free": p1_pe, "xtb_free": None,
                      "attnT_free": None, "stage_free": {}, "xres_free": {},
                      "nmb_free": None, "ps_a_free": p1_dv, "stats_free": None,
                      "bias_free": None, "qtb_free": {}, "wq_free": {},
                      "probs_free": None, "ps_cd_free": None, "sc_free": None}
                ev_xp_st = ev_xpb_st = None
                ln2_done = None

                def emit_slot(s_i):
                    nonlocal ev_xp_st, ev_xpb_st, ln2_done
                    L, V = SLOT_L[s_i], SLOT_V[s_i]
                    ev_mvar = q.dma("gpsimd", mvar[:], mvar_in[s_i, :, :],
                                    waits=[EV["mvar_free"], p1_dv if s_i == 0 else None,
                                           p1_sc if s_i == 0 else None],
                                    stream="mvar")

                    with contextlib.ExitStack() as pa:
                        ea = pa.enter_context
                        xtok = [ea(nc.sbuf_tensor(f"p2xt{s_i}_{i}", [128, D], F32))
                                for i in range(4)]
                        stage = ea(nc.sbuf_tensor(f"p2stage{s_i}", [128, 2 * QT], F32))
                        ps_tr = ea(nc.psum_tensor(f"p2ps_tr{s_i}", [128, 1024], F32))
                        ps_mu = ea(nc.psum_tensor(f"p2ps_mu{s_i}", [1, QT], F32))
                        ps_ss = ea(nc.psum_tensor(f"p2ps_ss{s_i}", [1, QT], F32))
                        ps_bc = ea(nc.psum_tensor(f"p2ps_bc{s_i}", [128, QT], F32))

                        for j in range(4):
                            q.dma("gpsimd", xtok[j][:],
                                  xq[s_i * QT + j * 128: s_i * QT + (j + 1) * 128, :],
                                  waits=[EV["xtok_free"],
                                         q.total("xps"), q.total("xpbs"),
                                         q.total("dbgs"),
                                         EV["stats_free"], EV["ps_a_free"],
                                         p1_sc if s_i == 0 and j == 0 else None,
                                         p1_dv if s_i == 0 and j == 0 else None],
                                  stream="xq")
                        xm_tot = q.total("xq")
                        ev_trevac, ev_stg, ev_stg_st = {}, {}, {}
                        for f in range(FC):
                            b0 = (f % 2) * 512
                            for j in range(4):
                                tr = q.do("tensor",
                                          lambda e_, f=f, j=j, b0=b0: e_.transpose(
                                              ps_tr[:, b0 + j * 128:b0 + (j + 1) * 128],
                                              xtok[j][:, f * 128:(f + 1) * 128], identf[:]),
                                          waits=[xm_tot,
                                                 (ev_trevac.get((f - 2, 3)) if j == 0 else None),
                                                 (ev_stg.get(f - 2) if j == 0 else None),
                                                 EV["ps_a_free"] if (f == 0 and j == 0) else None,
                                                 EV["xtb_free"] if f == 0 and j == 0 else None],
                                          inc=(1 if j == 3 else 0))
                            ev_trevac[(f, 3)] = q.do(
                                "scalar",
                                lambda e_, f=f, b0=b0: e_.copy(
                                    xtb[:, f * QT:(f + 1) * QT], ps_tr[:, b0:b0 + 512]),
                                waits=[tr])
                            if P2A < 2:
                                ev_stg[f] = ev_trevac[(f, 3)]
                                ev_stg_st[f] = q.total("stg")
                                continue
                            ev_stg[f] = q.do("scalar",
                                             lambda e_, f=f, b0=b0: e_.copy(
                                                 stage[:, (f % 2) * QT:(f % 2 + 1) * QT],
                                                 ps_tr[:, b0:b0 + 512]),
                                             waits=[tr,
                                                    q.total("stg") if f >= 2 else None,
                                                    q.total("xps") if f == 0 else None,
                                                    q.total("xpbs") if f == 0 else None])
                            ev_stg_st[f] = q.dma("gpsimd", xt_scr[f, :, :],
                                                 stage[:, (f % 2) * QT:(f % 2 + 1) * QT],
                                                 waits=[ev_stg[f], EV["stage_free"].get(f)],
                                                 stream="stg")
                        EV["xtok_free"] = ev_trevac[(FC - 1, 3)]
                        if "xtb" in dbg and s_i == 0:
                            q.dma("gpsimd", dbg["xtb"][:, :], xtb[:],
                                  waits=[ev_trevac[(FC - 1, 3)]], stream="dbgs")

                        if P2A < 3:
                            dummy = ev_trevac[(FC - 1, 3)]
                            EV["mvar_free"] = ev_mvar
                            EV["stats_free"] = dummy
                            EV["ps_a_free"] = dummy
                            EV["xtb_free"] = None
                            EV["stage_free"] = {f: ev_stg_st[f] for f in range(FC)}
                            ln2_done = dummy
                            ev_xp_st = dummy
                            ev_xpb_st = dummy
                            return
                        ev_sq, ev_ss_g = {}, {}
                        ev_mu = ev_ss = None
                        for f in range(FC):
                            ev_sq[f] = q.do("scalar",
                                            lambda e_, f=f: e_.activation(
                                                sqb[:, (f % 2) * QT:(f % 2 + 1) * QT],
                                                xtb[:, f * QT:(f + 1) * QT], AF.Square),
                                            waits=[ev_trevac[(f, 3)], ev_ss_g.get(f - 2)])
                            ev_mu = q.do("tensor",
                                         MM(ps_mu[:], onesb[:], xtb[:, f * QT:(f + 1) * QT],
                                            f == 0, f == FC - 1),
                                         waits=[ev_trevac[(f, 3)], ones_ev],
                                         inc=(1 if f == FC - 1 else 0))
                            ev_ss = q.do("tensor",
                                         MM(ps_ss[:], onesb[:],
                                            sqb[:, (f % 2) * QT:(f % 2 + 1) * QT],
                                            f == 0, f == FC - 1),
                                         waits=[ev_sq[f]])
                            ev_ss_g[f] = ev_ss

                        ev_srow = q.chain("vector", [
                            lambda e_: e_.tensor_scalar(murow[:], ps_mu[:], -1.0 / D, None, OP.mult),
                            lambda e_: e_.tensor_scalar(varrow[:], ps_ss[:], 1.0 / D, None, OP.mult),
                            lambda e_: e_.tensor_tensor(musq[:], murow[:], murow[:], OP.mult),
                            lambda e_: e_.tensor_tensor(varrow[:], varrow[:], musq[:], OP.subtract),
                            lambda e_: e_.tensor_copy(nmrow_b[:], murow[:]),
                        ], waits=[ev_mu, ev_ss, EV["stats_free"], EV["nmb_free"]])
                        ev_srow2 = ev_srow
                        ev_bc = q.do("tensor", MM(ps_bc[:], onesf[:], varrow[:], True, True),
                                     waits=[ev_srow, ones_ev])
                        ev_rbc = q.chain("scalar", [
                            lambda e_: e_.activation(rbc[:], ps_bc[:], AF.Ln, bias=epsc[:, 0:1]),
                            lambda e_: e_.activation(rbc[:], rbc[:], AF.Exp, scale=-0.5),
                        ], waits=[ev_bc, EV.get("rbc_dve_free")])
                        ev_rsbc = q.do("vector",
                                       lambda e_: e_.tensor_scalar(rsbc[:], rbc[:], SCALE,
                                                                   None, OP.mult),
                                       waits=[ev_rbc])
                        ev_a_done = (ev_rsbc, ev_srow, ev_trevac[(FC - 1, 3)])
                    if P2LEVEL < 2:
                        EV["mvar_free"] = ev_mvar
                        EV["stats_free"] = ev_srow
                        EV["ps_a_free"] = ev_srow
                        EV["nmb_free"] = None
                        EV["rbc_dve_free"] = ev_rsbc
                        EV["xtb_free"] = None
                        EV["stage_free"] = {f: ev_stg_st[f] for f in range(FC)}
                        ln2_done = ev_srow
                        ev_xp_st = q.total("stg")
                        ev_xpb_st = q.total("stg")
                        return

                    with contextlib.ExitStack() as pc:
                        ec = pc.enter_context
                        attnT = ec(nc.sbuf_tensor(f"p2attnT{s_i}", [128, NH * QT], BF16))
                        probs = [ec(nc.sbuf_tensor(f"p2pr{s_i}_{i}", [128, QT], BF16))
                                 for i in range(2)]
                        srow = ec(nc.sbuf_tensor(f"p2srow{s_i}", [1, QT], F32))
                        rbc_at = ec(nc.sbuf_tensor(f"p2rbcat{s_i}", [128, QT], F32))
                        ps_q = ec(nc.psum_tensor(f"p2ps_q{s_i}", [128, 2 * QT], F32))

                        with contextlib.ExitStack() as pcc:
                            ecc = pcc.enter_context
                            ps_sc = ecc(nc.psum_tensor(f"p2ps_sc{s_i}", [128, 2 * QT], F32))
                            ps_at = ecc(nc.psum_tensor(f"p2ps_at{s_i}", [128, QT], F32))
                            ps_sm = ecc(nc.psum_tensor(f"p2ps_sm{s_i}", [1, QT], F32))
                            ps_bca = ecc(nc.psum_tensor(f"p2ps_bca{s_i}", [128, QT], F32))

                            ev_at = {}
                            ev_qev = {}
                            ev_bias_free = EV["bias_free"]
                            ev_pv_prev = EV["probs_free"]
                            ev_bca = None
                            ev_rbca_free = None
                            ev_sm_free = None
                            ev_exp = {}
                            for h in range(NH):
                                wld = q.dma("sync", wstr[h % 2][:], wqg[h, :, :],
                                            waits=[EV["wq_free"].get(h - 2),
                                                   p1_sc if s_i == 0 and h == 0 else None,
                                                   p1_dv if s_i == 0 and h == 0 else None,
                                                   p1_pe if s_i == 0 and h == 0 else None],
                                            stream="wq", parity=h)
                                ev_corr = None
                                for f in range(FC):
                                    ev_corr = q.do(
                                        "tensor",
                                        MM(ps_q[:, (h % 2) * QT:(h % 2 + 1) * QT],
                                           wstr[h % 2][:, f * 128:(f + 1) * 128],
                                           xtb[:, f * QT:(f + 1) * QT], f == 0, False),
                                        waits=[wld, ev_trevac[(f, 3)],
                                               ev_qev.get(h - 2),
                                               (ev_a_done[0] if (h == 0 and f == 0) else None),
                                               (EV["ps_cd_free"] if (h == 0 and f == 0) else None)],
                                        inc=0)
                                ev_corr = q.do("tensor",
                                               MM(ps_q[:, (h % 2) * QT:(h % 2 + 1) * QT],
                                                  csq_sb[:, h * 128:(h + 1) * 128], nmrow_b[:],
                                                  False, True),
                                               waits=[ev_srow2, csq_ev])
                                EV["wq_free"][h] = ev_corr
                                ev_qev[h] = q.do("vector",
                                                 lambda e_, h=h: e_.tensor_tensor(
                                                     qtb[h % 2][:],
                                                     ps_q[:, (h % 2) * QT:(h % 2 + 1) * QT],
                                                     rsbc[:], OP.mult),
                                                 waits=[ev_corr, ev_rsbc,
                                                        EV["qtb_free"].get(h - 2),
                                                        q.total("dbgs")])
                                if "qt" in dbg and s_i == 0 and h < 2:
                                    q.dma("gpsimd", dbg["qt"][:, h * QT:(h + 1) * QT],
                                          qtb[h % 2][:], waits=[ev_qev[h]], stream="dbgs")

                                if P2LEVEL < 3:
                                    ev_at[h] = ev_qev[h]
                                    EV["qtb_free"][h] = ev_qev[h]
                                    ev_exp = {0: ev_qev[h]}
                                    nch = 1
                                    ev_pv_prev = ev_qev[h]
                                    ev_bias_free = ev_qev[h]
                                    continue
                                ev_bias = q.do("vector",
                                               lambda e_, h=h, s_i=s_i: e_.tensor_scalar(
                                                   bias_h[:], kqbsb[:, s_i * 32:(s_i + 1) * 32],
                                                   float(SLOPES[h]), STAB, OP.mult, OP.add),
                                               waits=[cst_ev, ev_bias_free])
                                kc0, kcL = head_sched(h, s_i)
                                nch = kcL - kc0
                                ev_pv = {}
                                for ci, kc in enumerate(range(kc0, kcL)):
                                    par = ci % 2
                                    smm = q.do("tensor",
                                               MM(ps_sc[:, par * QT:(par + 1) * QT],
                                                  ktsb[:, kc * 128:(kc + 1) * 128],
                                                  qtb[h % 2][:], True, True),
                                               waits=[ev_qev[h],
                                                      ev_kt_last if s_i == 0 and h == 0 else None,
                                                      ev_exp.get(ci - 2),
                                                      ev_bias_free if ci < 2 else None])
                                    tile = tdsb[:] if kc < V else \
                                        mvar[:, (kc - V) * QT:(kc - V + 1) * QT]
                                    amk = q.do("vector",
                                               lambda e_, tile=tile, h=h, par=par:
                                               e_.scalar_tensor_tensor(
                                                   ps_sc[:, par * QT:(par + 1) * QT], tile,
                                                   float(SLOPES[h]),
                                                   ps_sc[:, par * QT:(par + 1) * QT],
                                                   OP.mult, OP.add),
                                               waits=[smm, ev_mvar, ev_bias])
                                    ev_exp[ci] = q.do(
                                        "scalar",
                                        lambda e_, kc=kc, par=par: e_.activation(
                                            probs[par][:], ps_sc[:, par * QT:(par + 1) * QT],
                                            AF.Exp, bias=bias_h[:, kc:kc + 1], scale=1.0),
                                        waits=[amk, ev_pv.get(ci - 2),
                                               ev_pv_prev if ci < 2 else None])
                                    q.do("tensor",
                                         MM(ps_at[:], vsb[:, kc * 128:(kc + 1) * 128],
                                            probs[par][:], ci == 0, ci == nch - 1),
                                         waits=[ev_exp[ci],
                                                ev_vt_last if s_i == 0 and h == 0 else None,
                                                ev_at.get(h - 1),
                                                ev_sm_free if ci == 0 else None],
                                         inc=0)
                                    ev_pv[ci] = q.do("tensor",
                                                     MM(ps_sm[:], onesb[:], probs[par][:],
                                                        ci == 0, ci == nch - 1))
                                ev_bias_free = ev_exp[nch - 1]
                                ev_pv_prev = ev_pv[nch - 1]
                                EV["qtb_free"][h] = ev_pv[nch - 1]
                                ev_rec = q.do("vector",
                                              lambda e_: e_.reciprocal(srow[:], ps_sm[:]),
                                              waits=[ev_pv[nch - 1], ev_bca])
                                ev_sm_free = ev_rec
                                ev_bca = q.do("tensor", MM(ps_bca[:], onesf[:], srow[:],
                                                           True, True),
                                              waits=[ev_rec, ev_rbca_free])
                                ev_rbca = q.do("vector",
                                               lambda e_: e_.tensor_copy(rbc_at[:], ps_bca[:]),
                                               waits=[ev_bca])
                                ev_at[h] = q.do("vector",
                                                lambda e_, h=h: e_.tensor_tensor(
                                                    attnT[:, h * QT:(h + 1) * QT], ps_at[:],
                                                    rbc_at[:], OP.mult),
                                                waits=[ev_rbca,
                                                       EV["attnT_free"] if h == 0 else None,
                                                       q.total("dbgs") if h == 0 else None])
                                ev_rbca_free = ev_at[h]
                            EV["mvar_free"] = ev_exp[nch - 1]
                            EV["bias_free"] = ev_exp[nch - 1]
                            EV["probs_free"] = ev_pv_prev
                            ev_c_done = ev_at[NH - 1]
                            if "attnT" in dbg and s_i == 0:
                                q.dma("gpsimd", dbg["attnT"][:, :], attnT[:],
                                      waits=[ev_at[NH - 1]], stream="dbgs")

                        if P2LEVEL < 4:
                            EV["mvar_free"] = ev_exp[nch - 1]
                            EV["bias_free"] = ev_exp[nch - 1]
                            EV["probs_free"] = ev_pv_prev
                            EV["stats_free"] = ev_srow
                            EV["ps_a_free"] = ev_srow
                            EV["nmb_free"] = EV["wq_free"][NH - 1]
                            EV["rbc_dve_free"] = ev_rsbc
                            EV["xtb_free"] = EV["wq_free"][NH - 1]
                            EV["attnT_free"] = ev_at[NH - 1]
                            EV["stage_free"] = {f: ev_stg_st[f] for f in range(FC)}
                            ln2_done = ev_srow
                            ev_xp_st = ev_at[NH - 1]
                            ev_xpb_st = ev_at[NH - 1]
                            return
                        with contextlib.ExitStack() as pd:
                            ed = pd.enter_context
                            xres = [ed(nc.sbuf_tensor(f"p2xres{s_i}_{i}", [128, QT], F32))
                                    for i in range(2)]
                            xp = ed(nc.sbuf_tensor(f"p2xp{s_i}", [128, QT], F32))
                            xpb = ed(nc.sbuf_tensor(f"p2xpb{s_i}", [128, QT], BF16))
                            sq2 = ed(nc.sbuf_tensor(f"p2sq2{s_i}", [128, QT], BF16))
                            ps_mu2 = ed(nc.psum_tensor(f"p2ps_mu2{s_i}", [1, QT], F32))
                            ps_ss2 = ed(nc.psum_tensor(f"p2ps_ss2{s_i}", [1, QT], F32))

                            ev_wo_free, ev_xp, ev_xr = {}, {}, {}
                            ev_mu = ev_ss = None
                            ev_sq2 = None
                            for dp in range(FC):
                                wld = q.dma("sync", wstr[dp % 2][:], wog[dp, :, :],
                                            waits=[EV["wq_free"][NH - 1] if dp < 2 else
                                                   ev_wo_free.get(dp - 2)],
                                            stream="wq", parity=dp)
                                xr = q.dma("sync", xres[dp % 2][:], xt_scr[dp, :, :],
                                           waits=[q.total("stg"), EV["xres_free"].get(dp % 2)],
                                           stream="xr", parity=dp)
                                ev_xr[dp] = xr
                                mmk = None
                                for f in range(FC):
                                    mmk = q.do("tensor",
                                               MM(ps_q[:, (dp % 2) * QT:(dp % 2 + 1) * QT],
                                                  wstr[dp % 2][:, f * 128:(f + 1) * 128],
                                                  attnT[:, f * QT:(f + 1) * QT],
                                                  f == 0, f == FC - 1),
                                               waits=[wld, ev_at[f], ev_xp.get(dp - 2),
                                                      ev_c_done if dp == 0 and f == 0 else None,
                                                      ev_qev[NH - 1] if dp < 2 and f == 0 else None],
                                               inc=(1 if f == FC - 1 else 0))
                                ev_wo_free[dp] = mmk
                                ev_xp[dp] = q.do("vector",
                                                 lambda e_, dp=dp: e_.tensor_tensor(
                                                     xp[:],
                                                     ps_q[:, (dp % 2) * QT:(dp % 2 + 1) * QT],
                                                     xres[dp % 2][:], OP.add),
                                                 waits=[mmk, xr, q.total("xps"),
                                                        q.total("dbgs")])
                                EV["xres_free"][dp % 2] = ev_xp[dp]
                                ev_cst = q.do("scalar", lambda e_: e_.copy(xpb[:], xp[:]),
                                              waits=[ev_xp[dp], q.total("xpbs")])
                                ev_sq2 = q.do("scalar",
                                              lambda e_: e_.activation(sq2[:], xp[:], AF.Square),
                                              waits=[ev_ss if dp else None])
                                ev_xp_st = q.dma("gpsimd",
                                                 xprime[dp, :, s_i * QT:(s_i + 1) * QT], xp[:],
                                                 waits=[ev_xp[dp]], stream="xps")
                                if "xprime" in dbg:
                                    q.dma("gpsimd", dbg["xprime"][dp, :, s_i * QT:(s_i + 1) * QT],
                                          xp[:], stream="dbgs")
                                ev_xpb_st = q.dma("gpsimd",
                                                  xprime_b[dp, :, s_i * QT:(s_i + 1) * QT],
                                                  xpb[:], waits=[ev_cst], stream="xpbs")
                                ev_mu = q.do("tensor",
                                             MM(ps_mu2[:], onesb[:], xpb[:], dp == 0, dp == FC - 1),
                                             waits=[ev_cst,
                                                    ev_a_done[1] if dp == 0 else None],
                                             inc=(1 if dp == FC - 1 else 0))
                                ev_ss = q.do("tensor",
                                             MM(ps_ss2[:], onesb[:], sq2[:], dp == 0, dp == FC - 1),
                                             waits=[ev_sq2],
                                             inc=(1 if dp == FC - 1 else 0))
                            EV["attnT_free"] = ev_wo_free[FC - 1]
                            EV["wq_free"][-2] = ev_wo_free[FC - 2]
                            EV["wq_free"][-1] = ev_wo_free[FC - 1]
                            EV["xtb_free"] = EV["wq_free"][NH - 1]
                            EV["nmb_free"] = EV["wq_free"][NH - 1]
                            EV["rbc_dve_free"] = ev_rsbc
                            EV["stage_free"] = {f: ev_xr[f] for f in range(FC)}
                            EV["ps_cd_free"] = ev_xp[FC - 1]

                            ev_st2 = q.chain("vector", [
                                lambda e_: e_.tensor_scalar(murow[:], ps_mu2[:], -1.0 / D,
                                                            None, OP.mult),
                                lambda e_: e_.tensor_scalar(varrow[:], ps_ss2[:], 1.0 / D,
                                                            None, OP.mult),
                                lambda e_: e_.tensor_tensor(musq[:], murow[:], murow[:], OP.mult),
                                lambda e_: e_.tensor_tensor(varrow[:], varrow[:], musq[:],
                                                            OP.subtract),
                                lambda e_, s_i=s_i: e_.tensor_copy(
                                    nm2row[:, s_i * QT:(s_i + 1) * QT], murow[:]),
                                lambda e_, s_i=s_i: e_.tensor_copy(
                                    var2[:, s_i * QT:(s_i + 1) * QT], varrow[:]),
                            ], waits=[ev_mu, ev_ss, ev_srow2])
                            EV["stats_free"] = ev_st2
                            EV["ps_a_free"] = ev_st2
                            ln2_done = ev_st2


                for s_i in range(NSLOT):
                    emit_slot(s_i)
            p2_done = [ln2_done, q.total("xps"), q.total("xpbs"), q.total("stg"),
                       q.total("dbgs"),
                       ("tensor", q.sem["tensor"].n), ("vector", q.sem["vector"].n),
                       ("scalar", q.sem["scalar"].n)]
            return p2_done, ln2_done

        if upto == 'p1':
            q.do("gpsimd", lambda e_: e_.engine_nop(),
                 waits=[ev_kt_last, ev_vt_last, q.total("dbgs")], inc=0)
            p12.close()
            q.finalize(block)
            return nc, dbg
        p2_done, ln2_done = emit_p2()
        p12.close()

        def emit_p3():
            with contextlib.ExitStack() as ph:
                e2 = ph.enter_context
                xpb3 = e2(nc.sbuf_tensor("p3xpb", [128, FC * TOK], BF16))
                w1s = [e2(nc.sbuf_tensor(f"p3w1{i}", [128, FC * 128], BF16)) for i in range(2)]
                w2s = [e2(nc.sbuf_tensor(f"p3w2{i}", [128, FC * 128], BF16)) for i in range(2)]
                r2bc = e2(nc.sbuf_tensor("p3r2bc", [128, TOK], F32))
                gsc = e2(nc.sbuf_tensor("p3gsc", [128, TOK], F32))
                silu = e2(nc.sbuf_tensor("p3silu", [128, TOK], F32))
                hsb = [e2(nc.sbuf_tensor(f"p3h{i}", [128, TOK], BF16)) for i in range(2)]
                cs1g = [e2(nc.sbuf_tensor(f"p3cs1{i}", [1, 128], BF16)) for i in range(2)]
                cs2g = [e2(nc.sbuf_tensor(f"p3cs2{i}", [1, 128], BF16)) for i in range(2)]
                usc = gsc

                ev_xpb = None
                for f in range(FC):
                    ev_xpb = q.dma("sync", xpb3[:, f * TOK:(f + 1) * TOK], xprime_b[f, :, :],
                                   waits=p2_done if f == 0 else (), stream="xpb3")
                with contextlib.ExitStack() as pr:
                    er = pr.enter_context
                    ps_b2 = er(nc.psum_tensor("p3ps_b2", [128, QT], F32))
                    ev_r2bc = None
                    for s_i in range(NSLOT):
                        bmk = q.do("tensor",
                                   MM(ps_b2[:], onesf[:], var2[:, s_i * QT:(s_i + 1) * QT],
                                      True, True),
                                   waits=[ln2_done, ev_r2bc])
                        ev_r2bc = q.chain("scalar", [
                            lambda e_, s_i=s_i: e_.activation(
                                r2bc[:, s_i * QT:(s_i + 1) * QT], ps_b2[:],
                                AF.Ln, bias=epsc[:, 0:1]),
                            lambda e_, s_i=s_i: e_.activation(
                                r2bc[:, s_i * QT:(s_i + 1) * QT],
                                r2bc[:, s_i * QT:(s_i + 1) * QT], AF.Exp, scale=-0.5),
                        ], waits=[bmk])

                with contextlib.ExitStack() as pg:
                    eg = pg.enter_context
                    ps_g = eg(nc.psum_tensor("p3ps_g", [128, TOK], F32))
                    ps_u = eg(nc.psum_tensor("p3ps_u", [128, TOK], F32))
                    ev_gev, ev_uev, ev_hst = {}, {}, {}
                    ev_gmm, ev_umm = {}, {}
                    ev_sil_free, ev_h = {}, {}
                    ev_cs = {}
                    for g in range(HC):
                        w1ld = q.dma("sync", w1s[g % 2][:], w1t[g, :, :],
                                     waits=[ev_gmm.get(g - 2)], stream="w1", parity=g)
                        q.dma("gpsimd", cs1g[g % 2][:], cs1f[:, g * 128:(g + 1) * 128],
                              waits=[ev_gmm.get(g - 2)], stream="cs", parity=g)
                        ev_cs = q.dma("gpsimd", cs2g[g % 2][:], cs2f[:, g * 128:(g + 1) * 128],
                                      waits=[ev_umm.get(g - 2)], stream="cs", parity=g)
                        for f in range(FC):
                            for sp in range(4):
                                q.do("tensor",
                                     MM(ps_g[:, sp * QT:(sp + 1) * QT],
                                        w1s[g % 2][:, f * 128:(f + 1) * 128],
                                        xpb3[:, f * TOK + sp * QT: f * TOK + (sp + 1) * QT],
                                        f == 0, False),
                                     waits=[w1ld, ev_xpb, ev_gev.get(g - 1),
                                            ev_r2bc if g == 0 and f == 0 and sp == 0 else None],
                                     inc=0)
                        gmk = None
                        for sp in range(4):
                            gmk = q.do("tensor",
                                       MM(ps_g[:, sp * QT:(sp + 1) * QT],
                                          cs1g[g % 2][:],
                                          nm2row[:, sp * QT:(sp + 1) * QT], False, True),
                                       waits=[ev_cs, ln2_done],
                                       inc=(1 if sp == 3 else 0))
                        ev_gmm[g] = gmk
                        ev_gev[g] = q.do("vector",
                                         lambda e_: e_.tensor_tensor(gsc[:], ps_g[:], r2bc[:],
                                                                     OP.mult),
                                         waits=[gmk, ev_uev.get(g - 1)])
                        ev_sil = q.do("scalar", lambda e_: e_.activation(silu[:], gsc[:], AF.Silu),
                                      waits=[ev_gev[g], ev_h.get(g - 1)])
                        ev_sil_free[g] = ev_sil
                        w2ld = q.dma("sync", w2s[g % 2][:], w2t[g, :, :],
                                     waits=[ev_umm.get(g - 2)], stream="w2", parity=g)
                        for f in range(FC):
                            for sp in range(4):
                                q.do("tensor",
                                     MM(ps_u[:, sp * QT:(sp + 1) * QT],
                                        w2s[g % 2][:, f * 128:(f + 1) * 128],
                                        xpb3[:, f * TOK + sp * QT: f * TOK + (sp + 1) * QT],
                                        f == 0, False),
                                     waits=[w2ld, ev_uev.get(g - 1)],
                                     inc=0)
                        umk = None
                        for sp in range(4):
                            umk = q.do("tensor",
                                       MM(ps_u[:, sp * QT:(sp + 1) * QT],
                                          cs2g[g % 2][:],
                                          nm2row[:, sp * QT:(sp + 1) * QT], False, True),
                                       waits=[ev_cs],
                                       inc=(1 if sp == 3 else 0))
                        ev_umm[g] = umk
                        ev_uev[g] = q.do("vector",
                                         lambda e_: e_.tensor_tensor(usc[:], ps_u[:], r2bc[:],
                                                                     OP.mult),
                                         waits=[umk, ev_sil])
                        ev_h[g] = q.do("vector",
                                       lambda e_, g=g: e_.tensor_tensor(
                                           hsb[g % 2][:], silu[:], usc[:], OP.mult),
                                       waits=[ev_sil, ev_uev[g],
                                              q.total("hst") if g >= 2 else None,
                                              q.total("dbgs")])
                        ev_hst[g] = q.dma("gpsimd", h_scr[g, :, :], hsb[g % 2][:],
                                          waits=[ev_h[g]], stream="hst")
                        if "h" in dbg:
                            q.dma("gpsimd", dbg["h"][g, :, :], hsb[g % 2][:], stream="dbgs")
                p3_done = [q.total("hst"), q.total("dbgs"), ("tensor", q.sem["tensor"].n),
                           ("vector", q.sem["vector"].n), ("scalar", q.sem["scalar"].n)]
            return p3_done

        if upto == 'p2':
            q.do("gpsimd", lambda e_: e_.engine_nop(),
                 waits=[w for w in p2_done] + [q.total("dbgs")], inc=0)
            q.finalize(block)
            return nc, dbg
        p3_done = emit_p3()

        def emit_p4():
            with contextlib.ExitStack() as ph:
                e2 = ph.enter_context
                hsp = e2(nc.sbuf_tensor("p4hsp", [128, HC * QT], BF16))
                w3s = [e2(nc.sbuf_tensor(f"p4w3{i}", [128, HC * 128], BF16)) for i in range(2)]
                xres4 = [e2(nc.sbuf_tensor(f"p4xr{i}", [128, QT], F32)) for i in range(2)]
                outf = e2(nc.sbuf_tensor("p4outf", [128, QT], F32))
                outt = e2(nc.sbuf_tensor("p4outt", [128, QT], F32))
                ps_dn = e2(nc.psum_tensor("p4ps_dn", [128, 2 * QT], F32))
                ps_tr4 = e2(nc.psum_tensor("p4ps_tr", [128, 1024], F32))

                ev_out_st = None
                ev_otr_last = None
                ev_cpk = {}
                ev_dn_all = None
                ev_w3_free_prev = None
                ev_oev_prev = None
                for sp in range(4):
                    ev_h_ld = None
                    for g in range(HC):
                        ev_h_ld = q.dma("sync", hsp[:, g * QT:(g + 1) * QT],
                                        h_scr[g, :, sp * QT:(sp + 1) * QT],
                                        waits=(p3_done if (sp == 0 and g == 0) else
                                               ([ev_dn_all] if g == 0 else [])),
                                        stream="h4")
                    ev_w3_free, ev_dn, ev_oev = {}, {}, {}
                    for dp in range(FC):
                        w3ld = q.dma("sync", w3s[dp % 2][:], w3t[dp, :, :],
                                     waits=[ev_w3_free.get(dp - 2, ev_w3_free_prev)],
                                     stream="w3", parity=dp)
                        xr = q.dma("gpsimd", xres4[dp % 2][:],
                                   xprime[dp, :, sp * QT:(sp + 1) * QT],
                                   waits=[ev_oev.get(dp - 2, ev_oev_prev),
                                          p2_done[1] if sp == 0 and dp == 0 else None],
                                   stream="xr4", parity=dp)
                        mmk = None
                        for g in range(HC):
                            mmk = q.do("tensor",
                                       MM(ps_dn[:, (dp % 2) * QT:(dp % 2 + 1) * QT],
                                          w3s[dp % 2][:, g * 128:(g + 1) * 128],
                                          hsp[:, g * QT:(g + 1) * QT], g == 0, g == HC - 1),
                                       waits=[w3ld, ev_h_ld,
                                              ev_oev.get(dp - 2, ev_oev_prev)],
                                       inc=(1 if g == HC - 1 else 0))
                        ev_w3_free[dp] = mmk
                        ev_dn[dp] = mmk
                        oev = q.do("vector",
                                   lambda e_, dp=dp: e_.tensor_tensor(
                                       outf[:], ps_dn[:, (dp % 2) * QT:(dp % 2 + 1) * QT],
                                       xres4[dp % 2][:], OP.add),
                                   waits=[mmk, xr, ev_otr_last])
                        b0 = (dp % 2) * 512
                        for j in range(4):
                            tmk = q.do("tensor",
                                       lambda e_, j=j, b0=b0: e_.transpose(
                                           ps_tr4[:, b0 + j * 128:b0 + (j + 1) * 128],
                                           outf[:, j * 128:(j + 1) * 128], identf[:]),
                                       waits=[oev, ev_cpk.get(dp % 2) if j == 0 else None],
                                       inc=(1 if j == 3 else 0))
                        cpk = q.do("vector",
                                   lambda e_, b0=b0: e_.tensor_copy(outt[:], ps_tr4[:, b0:b0 + 512]),
                                   waits=[tmk, q.total("out")])
                        ev_cpk[dp % 2] = cpk
                        ev_otr_last = cpk
                        ev_oev[dp] = cpk
                        ev_out_st = q.dma(
                            "gpsimd",
                            out_q[sp * QT:(sp + 1) * QT, dp * 128:(dp + 1) * 128]
                            .rearrange("(j p) c -> p j c", p=128),
                            outt[:].rearrange("p (j c) -> p j c", c=128),
                            waits=[cpk], stream="out")
                    ev_dn_all = ev_dn[FC - 1]
                    ev_w3_free_prev = ev_w3_free[FC - 1]
                    ev_oev_prev = ev_oev[FC - 1]
                q.do("gpsimd", lambda e_: e_.engine_nop(),
                     waits=[q.total("out"), q.total("dbgs"), q.total("xps")], inc=0)

        if upto == 'p3':
            q.do("gpsimd", lambda e_: e_.engine_nop(),
                 waits=[w for w in p3_done] + [q.total("dbgs")], inc=0)
            q.finalize(block)
            return nc, dbg
        emit_p4()

        q.finalize(block)
    return nc, dbg


def _repack_grids(wqkv_b, wo_b, dq1, dq2, dq3):
    wqg = np.ascontiguousarray(
        wqkv_b.reshape(FC, 128, 34, 128).transpose(2, 1, 0, 3).reshape(34, 128, FC * 128))
    wog = np.ascontiguousarray(
        wo_b.reshape(FC, 128, FC, 128).transpose(2, 1, 0, 3).reshape(FC, 128, FC * 128))
    w1t = np.ascontiguousarray(
        dq1.reshape(HC, 128, FC, 128).transpose(0, 3, 2, 1).reshape(HC, 128, FC * 128))
    w2t = np.ascontiguousarray(
        dq2.reshape(HC, 128, FC, 128).transpose(0, 3, 2, 1).reshape(HC, 128, FC * 128))
    w3t = np.ascontiguousarray(
        dq3.reshape(FC, 128, HC, 128).transpose(0, 3, 2, 1).reshape(FC, 128, HC * 128))
    return wqg, wog, w1t, w2t, w3t


def _consts():
    ident = np.eye(128, dtype=np.float32)
    td = (np.arange(128, dtype=np.float32)[:, None]
          - np.arange(QT, dtype=np.float32)[None, :])
    return ident, np.ascontiguousarray(td)


def _core_masks(half):
    mvar = np.zeros((NSLOT, NVAR, 128, QT), dtype=np.float32)
    kqb = np.zeros((NSLOT, 128, 32), dtype=np.float32)
    p = np.arange(128)[:, None]
    qi = np.arange(QT)[None, :]
    for s_i in range(NSLOT):
        q0 = SLOT_Q0[s_i][half]
        span = q0 // 128 + 4
        L, V = SLOT_L[s_i], SLOT_V[s_i]
        for vix in range(NVAR):
            j = V + vix
            k0 = j * 128
            base = (p - qi).astype(np.float32)
            if j >= span:
                mvar[s_i, vix] = base - 1e30
            else:
                valid = (k0 + p) <= (q0 + qi)
                mvar[s_i, vix] = base + np.where(valid, 0.0, -1e30).astype(np.float32)
        for j in range(32):
            if j < min(L, span):
                kqb[s_i, :, j] = np.float32(j * 128 - q0)
    mvar_dev = np.ascontiguousarray(
        mvar.transpose(0, 2, 1, 3).reshape(NSLOT, 128, NVAR * QT))
    kqb_dev = np.ascontiguousarray(kqb.transpose(1, 0, 2).reshape(128, NSLOT * 32))
    return mvar_dev, kqb_dev


def kernel(L0_attn_hidden_states, L0_attn_Wqkv, L0_attn_Wo, L0_ffn_w1, L0_ffn_w2,
           L0_ffn_w3):
    x = np.ascontiguousarray(np.asarray(L0_attn_hidden_states, dtype=np.float32))
    wqkv = np.ascontiguousarray(np.asarray(L0_attn_Wqkv, dtype=np.float32))
    wo = np.ascontiguousarray(np.asarray(L0_attn_Wo, dtype=np.float32))
    w1 = np.ascontiguousarray(np.asarray(L0_ffn_w1, dtype=np.float32))
    w2 = np.ascontiguousarray(np.asarray(L0_ffn_w2, dtype=np.float32))
    w3 = np.ascontiguousarray(np.asarray(L0_ffn_w3, dtype=np.float32))

    ncA = build_prep()
    in_A = []
    for c in range(8):
        r0 = sum(W12_CHUNKS[:c]) * 128
        rows = W12_CHUNKS[c] * 128
        w1s = np.ones((W12_ROWS, D), dtype=np.float32)
        w1s[:rows] = w1[r0:r0 + rows]
        w2s = np.ones((W12_ROWS, D), dtype=np.float32)
        w2s[:rows] = w2[r0:r0 + rows]
        in_A.append({
            "wqkv_s": wqkv[c * 512:(c + 1) * 512],
            "wo_s": wo[c * 512:(c + 1) * 512],
            "w1_s": w1s, "w2_s": w2s,
            "w3_s": w3[c * 512:(c + 1) * 512],
        })
    resA = run_bass_kernel_spmd(ncA, in_A, core_ids=list(range(8)))
    ra = resA.results

    wqkv_b = np.concatenate([np.asarray(ra[c]["wqkv_b"]) for c in range(8)], axis=0)
    wo_b = np.concatenate([np.asarray(ra[c]["wo_b"]) for c in range(8)], axis=0)
    dq1 = np.concatenate([np.asarray(ra[c]["dq1"])[:W12_CHUNKS[c] * 128] for c in range(8)], axis=0)
    dq2 = np.concatenate([np.asarray(ra[c]["dq2"])[:W12_CHUNKS[c] * 128] for c in range(8)], axis=0)
    dq3 = np.concatenate([np.asarray(ra[c]["dq3"]) for c in range(8)], axis=0)
    rs1 = np.concatenate([np.asarray(ra[c]["rs1"])[:W12_CHUNKS[c] * 128] for c in range(8)], axis=0)
    rs2 = np.concatenate([np.asarray(ra[c]["rs2"])[:W12_CHUNKS[c] * 128] for c in range(8)], axis=0)
    cspart = np.stack([np.asarray(ra[c]["csp"])[0] for c in range(8)], axis=0)

    wqg, wog, w1t, w2t, w3t = _repack_grids(wqkv_b, wo_b, dq1, dq2, dq3)
    cs1f = np.ascontiguousarray(rs1.reshape(1, HC * 128))
    cs2f = np.ascontiguousarray(rs2.reshape(1, HC * 128))

    ncB, _dbg = build_main()
    ident, td = _consts()
    in_B = []
    for c in range(8):
        b, half = c // 2, c % 2
        q0s = [SLOT_Q0[s][half] for s in range(NSLOT)]
        xq = np.ascontiguousarray(np.concatenate([x[b, q0:q0 + QT] for q0 in q0s], axis=0))
        mvar, kqb = _core_masks(half)
        in_B.append({
            "xfull": x[b], "xq": xq,
            "td_in": td, "mvar_in": mvar, "kqb_in": kqb,
            "idf": ident, "idb": ident.astype(ml_dtypes.bfloat16),
            "wqg": wqg, "wog": wog, "w1t": w1t, "w2t": w2t, "w3t": w3t,
            "cspart": cspart, "cs1f": cs1f, "cs2f": cs2f,
        })
    resB = run_bass_kernel_spmd(ncB, in_B, core_ids=list(range(8)))

    out = np.empty((B, S, D), dtype=np.float32)
    for c in range(8):
        b, half = c // 2, c % 2
        oq = np.asarray(resB.results[c]["out_q"])
        for s_i in range(NSLOT):
            q0 = SLOT_Q0[s_i][half]
            out[b, q0:q0 + QT] = oq[s_i * QT:(s_i + 1) * QT]
    return out
